# version 03
# baseline (speedup 1.0000x reference)
import numpy as np
import ml_dtypes
import concourse.bass as bass
import concourse.mybir as mybir
from concourse.bass_utils import run_bass_kernel_spmd
from contextlib import ExitStack

F32 = mybir.dt.float32
BF16 = mybir.dt.bfloat16
I32 = mybir.dt.int32
U8 = mybir.dt.uint8
ALU = mybir.AluOpType
AF = mybir.ActivationFunctionType
AX = mybir.AxisListType

D = 4096
INTER = 11008
T = 1024
NTT = 8
NCH = 8
KT = 32
ICS = [(512 * i, 512) for i in range(21)] + [(10752, 256)]
NIC = len(ICS)
NM2 = 11
LOMASK = 0x0F0F0F0F
EPS = 1e-5
SEMS = ("v", "a", "pe",
        "gpat", "gx0", "gx1", "gw0", "gw1", "gm0", "gm1", "gz",
        "sxb0", "sxb1", "sxt", "swb0", "swb1",
        "swr0", "swr1", "swr2", "swr3", "swr4", "swr5", "swr6", "swr7",
        "cxj0", "cxj1", "crs0", "crs1")


class B:

    def __init__(self, dry, ms):
        self.dry = dry
        self.ms = ms
        self.c = {k: 0 for k in SEMS}
        self.h = {}
        self.cur = None
        self.eng = None
        self.auto_sw = False

    def op(self, f, sem=None, n=1):
        if sem is None:
            sem = self.cur
        assert sem is not None, "op without a semaphore (gpsimd/sync compute?)"
        if not self.dry:
            if self.auto_sw and self.cur is not None and self.c[self.cur] > 0:
                self.eng.wait_ge(self.h[self.cur], self.c[self.cur])
            ins = f()
            if sem is not None:
                ins.then_inc(self.h[sem], n)
        if sem is not None:
            self.c[sem] += n

    def dma(self, f, sem="?"):
        self.op(f, sem, 16)

    def mark(self, key, sem):
        if self.dry:
            self.ms[key] = (sem, self.c[sem])
        else:
            assert self.ms[key] == (sem, self.c[sem]), (key, self.ms[key], sem, self.c[sem])

    def wait(self, eng, key):
        if self.dry:
            return
        sem, v = self.ms[key]
        if v > 0:
            eng.wait_ge(self.h[sem], v)

    def barrier_mark(self, tag):
        for s in SEMS:
            self.mark(f"bar_{tag}_{s}", s)

    def barrier_wait(self, eng, tag):
        for s in SEMS:
            self.wait(eng, f"bar_{tag}_{s}")


_CSEM = {"gpsimd": None, "sync": None, "scalar": "a", "vector": "v", "tensor": "pe"}


def _sections(b, blk, items):
    for name, fn in items:
        b.cur = _CSEM[name]
        b.auto_sw = name in ("vector", "scalar")
        if b.dry:
            fn(None)
        else:
            def _wrap(eng, fn=fn, name=name):
                b.cur = _CSEM[name]
                b.auto_sw = name in ("vector", "scalar")
                b.eng = eng
                fn(eng)
            getattr(blk, name)(_wrap)
        b.cur = None
        b.auto_sw = False


def _build(debug=False, stop=None):
    nc = bass.Bass("TRN2", target_bir_lowering=False)

    X = nc.dram_tensor("x", [T, D], F32, kind="ExternalInput").ap()
    W = nc.dram_tensor("w", [D, D], F32, kind="ExternalInput").ap()
    MK = nc.dram_tensor("mk", [D, D], I32, kind="ExternalInput").ap()
    GQ = nc.dram_tensor("gq", [512, INTER], I32, kind="ExternalInput").ap()
    UQ = nc.dram_tensor("uq", [512, INTER], I32, kind="ExternalInput").ap()
    DQ = nc.dram_tensor("dq", [1376, D], I32, kind="ExternalInput").ap()
    GS = nc.dram_tensor("gs", [512, INTER], F32, kind="ExternalInput").ap()
    US = nc.dram_tensor("us", [512, INTER], F32, kind="ExternalInput").ap()
    DS = nc.dram_tensor("ds", [1376, D], F32, kind="ExternalInput").ap()
    GZ = nc.dram_tensor("gz", [32, 1376], I32, kind="ExternalInput").ap()
    UZ = nc.dram_tensor("uz", [32, 1376], I32, kind="ExternalInput").ap()
    DZ = nc.dram_tensor("dz", [86, 512], I32, kind="ExternalInput").ap()
    GSR = nc.dram_tensor("gsr", [32, INTER], F32, kind="ExternalInput").ap()
    USR = nc.dram_tensor("usr", [32, INTER], F32, kind="ExternalInput").ap()
    DSR = nc.dram_tensor("dsr", [86, D], F32, kind="ExternalInput").ap()
    PID = nc.dram_tensor("pid", [128, 128], BF16, kind="ExternalInput").ap()
    SEL = nc.dram_tensor("sel", [128, KT * 128], BF16, kind="ExternalInput").ap()
    Y = nc.dram_tensor("y", [T, D], F32, kind="ExternalOutput").ap()

    dk = "ExternalOutput" if debug else "Internal"
    WBF = nc.dram_tensor("wbf", [D, D], BF16, kind="Internal").ap()
    XBF = nc.dram_tensor("xbf", [T, D], BF16, kind="Internal").ap()
    RSP = nc.dram_tensor("rsp", [T, D], F32, kind=dk).ap()
    X1BF = nc.dram_tensor("x1bf", [T, D], BF16, kind=dk).ap()
    HSP = nc.dram_tensor("hsp", [NM2 * 1024, 1024], BF16, kind=dk).ap()

    ms = {}
    for dry in (True, False):
        b = B(dry, ms)
        with ExitStack() as root:
            en = root.enter_context
            if not dry:
                for s in SEMS:
                    b.h[s] = en(nc.semaphore(s))
                pid = en(nc.sbuf_tensor("pid_sb", [128, 128], BF16))
                xgsT = en(nc.sbuf_tensor("xgsT", [32, T], BF16))
                hgsT = en(nc.sbuf_tensor("hgsT", [86, T], BF16))
                xg_tok = en(nc.sbuf_tensor("xg_tok", [128, 32 * NTT], F32))
                hg_tok = en(nc.sbuf_tensor("hg_tok", [128, 88 * NTT], F32))
                ggb = en(nc.sbuf_tensor("ggb", [128, 128], BF16))
                szc_d = en(nc.sbuf_tensor("szc_d", [86, D], BF16))
                ST = {k: en(nc.sbuf_tensor("st_" + k, [128, NTT], F32))
                      for k in ("sum", "sq", "mean", "msq", "ve", "rec", "i0", "ta", "inv", "nsb")}
                blk = en(nc.Block())
            else:
                pid = xgsT = hgsT = xg_tok = hg_tok = ggb = szc_d = None
                ST = {k: None for k in ("sum", "sq", "mean", "msq", "ve", "rec", "i0", "ta", "inv", "nsb")}
                blk = None

            def SC(k, t):
                return ST[k][:, t:t + 1]

            with ExitStack() as sa:
                if not dry:
                    ea = sa.enter_context
                    wstg = ea(nc.sbuf_tensor("wstg", [128, 2 * D], BF16))
                    xstg = ea(nc.sbuf_tensor("xstg", [128, 2 * D], BF16))
                    mstg = ea(nc.sbuf_tensor("mstg", [128, 2 * D], F32))
                    csf = ea(nc.sbuf_tensor("csf", [128, 128], F32))
                    csb = ea(nc.sbuf_tensor("csb", [128, 2 * 128], BF16))
                    bmc = ea(nc.sbuf_tensor("bmc", [128, 2 * 4], F32))
                    bmT01 = ea(nc.sbuf_tensor("bmT01", [128, 128], BF16))
                    sel_sb = ea(nc.sbuf_tensor("sel_sb", [128, KT * 128], BF16))
                    xT = ea(nc.sbuf_tensor("xT", [128, KT * 1024], BF16))
                    wmraw = ea(nc.sbuf_tensor("wmraw", [128, 8 * 512], BF16))
                    wmm = ea(nc.sbuf_tensor("wmm", [128, KT * 512], BF16))
                    erep = ea(nc.sbuf_tensor("erep", [128, 2 * 512], BF16))
                    xjit = ea(nc.sbuf_tensor("xjit", [128, 2 * 512], F32))
                    rstg = ea(nc.sbuf_tensor("rstg", [128, 2 * 512], F32))
                    psa = [ea(nc.psum_tensor(f"psa{i}", [128, 512], F32)) for i in range(4)]
                    pse = [ea(nc.psum_tensor(f"pse{i}", [128, 512], F32)) for i in range(2)]
                    pst = [ea(nc.psum_tensor(f"pst{i}", [128, 128], BF16)) for i in range(2)]

                def gA(g):
                    b.dma(lambda: g.dma_start(out=pid[:], in_=PID[:]), "gpat")
                    b.dma(lambda: g.dma_start(out=sel_sb[:], in_=SEL[:]), "gpat")
                    b.mark("pats", "gpat")
                    for i in range(NTT):
                        sl = i % 2
                        if i >= 2:
                            b.wait(g, f"xbs{i-2}")
                        b.dma(lambda sl=sl, i=i: g.dma_start(
                            out=xstg[:, sl * D:(sl + 1) * D], in_=X[i * 128:(i + 1) * 128, :]), f"gx{sl}")
                        b.mark(f"xc{i}", f"gx{sl}")
                    for nt in range(KT):
                        sl = nt % 2
                        if nt >= 2:
                            b.wait(g, f"wbs{nt-2}")
                        b.dma(lambda sl=sl, nt=nt: g.dma_start(
                            out=wstg[:, sl * D:(sl + 1) * D], in_=W[nt * 128:(nt + 1) * 128, :]), f"gw{sl}")
                        b.mark(f"wc{nt}", f"gw{sl}")
                        if nt >= 2:
                            b.wait(g, f"mrd{nt-2}")
                        b.dma(lambda sl=sl, nt=nt: g.dma_start(
                            out=mstg[:, sl * D:(sl + 1) * D], in_=MK[nt * 128:(nt + 1) * 128, :]), f"gm{sl}")
                        b.mark(f"mk{nt}", f"gm{sl}")

                def sA(s):
                    for i in range(NTT):
                        b.wait(s, f"xc{i}")
                        sl = i % 2
                        b.dma(lambda sl=sl, i=i: s.dma_start(
                            out=XBF[i * 128:(i + 1) * 128, :], in_=xstg[:, sl * D:(sl + 1) * D]), f"sxb{sl}")
                        b.mark(f"xbs{i}", f"sxb{sl}")
                    for kt in range(KT):
                        if kt == 0:
                            b.wait(s, f"xbs{NTT-2}")
                            b.wait(s, f"xbs{NTT-1}")
                        b.dma(lambda kt=kt: s.dma_start(
                            out=xT[:, kt * 1024:(kt + 1) * 1024],
                            in_=XBF[:, kt * 128:(kt + 1) * 128], transpose=True), "sxt")
                        b.mark(f"xT{kt}", "sxt")
                    for nc_i in range(NCH):
                        for q in range(4):
                            nt = 4 * nc_i + q
                            b.wait(s, f"wc{nt}")
                            sl = nt % 2
                            b.dma(lambda sl=sl, nt=nt: s.dma_start(
                                out=WBF[nt * 128:(nt + 1) * 128, :], in_=wstg[:, sl * D:(sl + 1) * D]), f"swb{sl}")
                            b.mark(f"wbs{nt}", f"swb{sl}")
                        for kt in range(KT):
                            idx = nc_i * KT + kt
                            r = idx % 8
                            if idx >= 8:
                                b.wait(s, f"wmul{idx-8}")
                            if kt == 0:
                                b.wait(s, f"wbs{4*nc_i+2}")
                                b.wait(s, f"wbs{4*nc_i+3}")
                            b.dma(lambda r=r, nc_i=nc_i, kt=kt: s.dma_start(
                                out=wmraw[:, r * 512:(r + 1) * 512],
                                in_=WBF[nc_i * 512:(nc_i + 1) * 512, kt * 128:(kt + 1) * 128],
                                transpose=True), f"swr{r}")
                            b.mark(f"wraw{idx}", f"swr{r}")

                def cA(c):
                    for nc_i in range(NCH):
                        for t in range(NTT):
                            idx = nc_i * NTT + t
                            sl = idx % 2
                            if idx >= 2:
                                b.wait(c, f"aevac{idx-2}")
                            b.dma(lambda sl=sl, nc_i=nc_i, t=t: c.dma_start(
                                out=xjit[:, sl * 512:(sl + 1) * 512],
                                in_=X[t * 128:(t + 1) * 128, nc_i * 512:(nc_i + 1) * 512]), f"cxj{sl}")
                            b.mark(f"xj{idx}", f"cxj{sl}")
                            b.wait(c, f"aevac{idx}")
                            b.dma(lambda sl=sl, nc_i=nc_i, t=t: c.dma_start(
                                out=RSP[t * 128:(t + 1) * 128, nc_i * 512:(nc_i + 1) * 512],
                                in_=rstg[:, sl * 512:(sl + 1) * 512]), f"crs{sl}")
                            b.mark(f"rst{idx}", f"crs{sl}")

                def vA(v):
                    for nc_i in range(NCH):
                        for q in range(4):
                            nt = 4 * nc_i + q
                            sl = nt % 2
                            b.wait(v, f"mk{nt}")
                            b.op(lambda sl=sl: v.tensor_reduce(
                                csf[:], mstg[:, sl * D:(sl + 1) * D].rearrange("p (a b) -> p a b", b=32),
                                AX.X, ALU.add))
                            b.mark(f"mrd{nt}", "v")
                            if nt >= 2:
                                b.wait(v, f"mtr{nt-2}")
                            b.op(lambda sl=sl: v.tensor_copy(csb[:, sl * 128:(sl + 1) * 128], csf[:]))
                            b.mark(f"mcv{nt}", "v")
                            b.wait(v, f"mtr{nt}")
                            b.op(lambda sl=sl: v.tensor_reduce(
                                bmc[:, sl * 4:(sl + 1) * 4],
                                pst[sl][:].rearrange("p (a b) -> p a b", b=32), AX.X, ALU.add))
                            b.op(lambda sl=sl, nt=nt: v.tensor_single_scalar(
                                bmT01[:, nt * 4:(nt + 1) * 4], bmc[:, sl * 4:(sl + 1) * 4], 0.0, ALU.is_gt))
                            b.mark(f"bm{nt}", "v")
                        sl = nc_i % 2
                        if nc_i >= 2:
                            b.wait(v, f"expl{nc_i-2}")
                        b.op(lambda sl=sl, nc_i=nc_i: v.tensor_copy(
                            erep[:, sl * 512:(sl + 1) * 512].rearrange("p (a b) -> p a b", b=32),
                            bmT01[:, nc_i * 16:(nc_i + 1) * 16].unsqueeze(2).broadcast_to([128, 16, 32])))
                        b.mark(f"erep{nc_i}", "v")
                        for kt in range(KT):
                            idx = nc_i * KT + kt
                            b.wait(v, f"wraw{idx}")
                            b.wait(v, f"exp{idx}")
                            if nc_i >= 1 and kt == 0:
                                b.wait(v, f"amm{nc_i-1}_{NTT-1}")
                            b.op(lambda idx=idx, kt=kt: v.tensor_tensor(
                                wmm[:, kt * 512:(kt + 1) * 512],
                                wmraw[:, (idx % 8) * 512:((idx % 8) + 1) * 512],
                                pse[idx % 2][:], ALU.mult))
                            b.mark(f"wmul{idx}", "v")
                        for t in range(NTT):
                            idx = nc_i * NTT + t
                            sl = idx % 2
                            b.wait(v, f"amm{nc_i}_{t}")
                            b.wait(v, f"xj{idx}")
                            if idx >= 2:
                                b.wait(v, f"rst{idx-2}")
                            b.op(lambda sl=sl, t=t: v.tensor_tensor(
                                rstg[:, sl * 512:(sl + 1) * 512], psa[t % 4][:],
                                xjit[:, sl * 512:(sl + 1) * 512], ALU.add))
                            b.mark(f"aevac{idx}", "v")

                def tA(t_):
                    b.wait(t_, "pats")
                    for nc_i in range(NCH):
                        for q in range(4):
                            nt = 4 * nc_i + q
                            sl = nt % 2
                            b.wait(t_, f"mcv{nt}")
                            if nt >= 2:
                                b.wait(t_, f"bm{nt-2}")
                            b.op(lambda sl=sl: t_.transpose(pst[sl][:], csb[:, sl * 128:(sl + 1) * 128], pid[:]))
                            b.mark(f"mtr{nt}", "pe")
                        b.wait(t_, f"erep{nc_i}")
                        for kt in range(KT):
                            idx = nc_i * KT + kt
                            if idx >= 2:
                                b.wait(t_, f"wmul{idx-2}")
                            b.op(lambda idx=idx, kt=kt, nc_i=nc_i: t_.matmul(
                                pse[idx % 2][:], sel_sb[:, kt * 128:(kt + 1) * 128],
                                erep[:, (nc_i % 2) * 512:((nc_i % 2) + 1) * 512],
                                start=True, stop=True))
                            b.mark(f"exp{idx}", "pe")
                        b.mark(f"expl{nc_i}", "pe")
                        for t in range(NTT):
                            idx = nc_i * NTT + t
                            if idx >= 4:
                                b.wait(t_, f"aevac{idx-4}")
                            for k in range(KT):
                                if t == 0:
                                    b.wait(t_, f"wmul{nc_i*KT+k}")
                                    if nc_i == 0 and k == 0:
                                        b.wait(t_, f"xT{KT-1}")
                                b.op(lambda t=t, k=k: t_.matmul(
                                    psa[t % 4][:],
                                    xT[:, k * 1024 + t * 128:k * 1024 + (t + 1) * 128],
                                    wmm[:, k * 512:(k + 1) * 512],
                                    start=(k == 0), stop=(k == KT - 1)),
                                    sem="pe" if k == KT - 1 else None)
                            b.mark(f"amm{nc_i}_{t}", "pe")

                _sections(b, blk, [("gpsimd", gA), ("sync", sA), ("scalar", cA),
                                   ("vector", vA), ("tensor", tA)])
                b.barrier_mark("p1")

            if stop == "p1":
                def finp1(g):
                    b.barrier_wait(g, "p1")
                _sections(b, blk, [("gpsimd", finp1)])
                continue

            with ExitStack() as s23:
                if not dry:
                    e23 = s23.enter_context
                    szc_g = e23(nc.sbuf_tensor("szc_g", [32, INTER], BF16))
                    szc_u = e23(nc.sbuf_tensor("szc_u", [32, INTER], BF16))
                else:
                    szc_g = szc_u = None

                with ExitStack() as s2:
                    if not dry:
                        e2 = s2.enter_context
                        Rt = e2(nc.sbuf_tensor("Rt", [128, 2 * D], F32))
                        sqs = e2(nc.sbuf_tensor("sqs", [128, D], F32))
                        x1p = e2(nc.sbuf_tensor("x1p", [128, 2 * D], BF16))
                        xgs1 = e2(nc.sbuf_tensor("xgs1", [128, 256], F32))
                        sraw = e2(nc.sbuf_tensor("sraw", [128, INTER], F32))
                        zt = e2(nc.sbuf_tensor("zt", [128, 1376], I32))
                        zlo = e2(nc.sbuf_tensor("zlo", [128, 1376], I32))
                        zhi = e2(nc.sbuf_tensor("zhi", [128, 1376], I32))
                        ps_xg = e2(nc.psum_tensor("ps_xg", [32, 128], BF16))
                    else:
                        Rt = sqs = x1p = xgs1 = sraw = zt = zlo = zhi = ps_xg = None

                    def g2(g):
                        b.barrier_wait(g, "p1")
                        b.dma(lambda: g.dma_start(out=zt[0:32, 0:1376], in_=GZ[:]), "gz")
                        b.dma(lambda: g.dma_start(out=sraw[0:32, :], in_=GSR[:]), "gz")
                        b.mark("zg", "gz")
                        b.wait(g, "szc_g")
                        b.dma(lambda: g.dma_start(out=zt[0:32, 0:1376], in_=UZ[:]), "gz")
                        b.dma(lambda: g.dma_start(out=sraw[0:32, :], in_=USR[:]), "gz")
                        b.mark("zu", "gz")
                        b.wait(g, "szc_u")
                        b.dma(lambda: g.dma_start(out=zt[0:86, 0:512], in_=DZ[:]), "gz")
                        b.dma(lambda: g.dma_start(out=sraw[0:86, 0:D], in_=DSR[:]), "gz")
                        b.mark("zd", "gz")

                    def ca2(c):
                        b.barrier_wait(c, "p1")
                        for t in range(NTT):
                            sl = t % 2
                            if t >= 2:
                                b.wait(c, f"xgr{t-2}")
                                b.wait(c, f"x1s{t-2}")
                                b.wait(c, f"st1_{t-2}")
                            b.dma(lambda sl=sl, t=t: c.dma_start(
                                out=Rt[:, sl * D:(sl + 1) * D], in_=RSP[t * 128:(t + 1) * 128, :]), f"cxj{sl}")
                            b.mark(f"rld{t}", f"cxj{sl}")
                            b.wait(c, f"rld{t}")
                            b.op(lambda sl=sl, t=t: c.activation(
                                sqs[:], Rt[:, sl * D:(sl + 1) * D], AF.Square, accum_out=SC("sq", t)))
                            b.mark(f"sq{t}", "a")
                            b.wait(c, f"st1_{t}")
                            b.op(lambda t=t: c.activation(SC("i0", t), SC("rec", t), AF.Sqrt))
                            b.mark(f"sqrt{t}", "a")

                    def v2(v):
                        b.barrier_wait(v, "p1")
                        for t in range(NTT):
                            sl = t % 2
                            b.wait(v, f"sq{t}")
                            b.op(lambda sl=sl, t=t: v.reduce_sum(SC("sum", t), Rt[:, sl * D:(sl + 1) * D], AX.X))
                            b.op(lambda t=t: v.tensor_scalar_mul(SC("mean", t), SC("sum", t), 1.0 / D))
                            b.op(lambda t=t: v.tensor_mul(SC("msq", t), SC("mean", t), SC("mean", t)))
                            b.op(lambda t=t: v.scalar_tensor_tensor(
                                SC("ve", t), SC("sq", t), 1.0 / D, SC("msq", t), ALU.mult, ALU.subtract))
                            b.op(lambda t=t: v.tensor_scalar_add(SC("ve", t), SC("ve", t), EPS))
                            b.op(lambda t=t: v.reciprocal(SC("rec", t), SC("ve", t)))
                            b.mark(f"st1_{t}", "v")
                            b.wait(v, f"sqrt{t}")
                            b.op(lambda t=t: v.tensor_mul(SC("ta", t), SC("i0", t), SC("i0", t)))
                            b.op(lambda t=t: v.tensor_mul(SC("ta", t), SC("ta", t), SC("ve", t)))
                            b.op(lambda t=t: v.tensor_scalar(SC("ta", t), SC("ta", t), -0.5, 1.5, ALU.mult, ALU.add))
                            b.op(lambda t=t: v.tensor_mul(SC("inv", t), SC("i0", t), SC("ta", t)))
                            b.op(lambda t=t: v.tensor_scalar_mul(SC("nsb", t), SC("mean", t), -1.0))
                            b.mark(f"st2_{t}", "v")
                            b.op(lambda sl=sl, t=t: v.tensor_scalar(
                                x1p[:, sl * D:(sl + 1) * D].rearrange("p (v u) -> p u v", u=512),
                                Rt[:, sl * D:(sl + 1) * D].rearrange("p (u v) -> p u v", v=8),
                                SC("nsb", t), SC("inv", t), ALU.add, ALU.mult))
                            b.mark(f"x1a{t}", "v")
                            b.op(lambda sl=sl: v.tensor_reduce(
                                xgs1[:].rearrange("p (g v) -> p g v", v=8),
                                x1p[:, sl * D:(sl + 1) * D].rearrange("p (v g w) -> p g v w", g=32, w=16),
                                AX.X, ALU.add))
                            b.op(lambda t=t: v.tensor_reduce(
                                xg_tok[:, t * 32:(t + 1) * 32],
                                xgs1[:].rearrange("p (g v) -> p g v", v=8), AX.X, ALU.add))
                            b.mark(f"xgr{t}", "v")
                            if t >= 1:
                                b.wait(v, f"xgt{t-1}")
                            b.op(lambda t=t: v.tensor_copy(ggb[:, 0:32], xg_tok[:, t * 32:(t + 1) * 32]))
                            b.mark(f"xgb{t}", "v")
                            b.wait(v, f"xgt{t}")
                            b.op(lambda t=t: v.tensor_scalar_mul(
                                xgsT[:, t * 128:(t + 1) * 128], ps_xg[:], -1.0))
                            b.mark(f"xgc{t}", "v")
                        for name, mkey, dst, nzc, wid in (
                                ("szc_g", "zg", 0, 32, 1376), ("szc_u", "zu", 1, 32, 1376),
                                ("szc_d", "zd", 2, 86, 512)):
                            b.wait(v, mkey)
                            b.op(lambda nzc=nzc, wid=wid: v.tensor_scalar(
                                zlo[0:nzc, 0:wid], zt[0:nzc, 0:wid], LOMASK, None, ALU.bitwise_and))
                            b.op(lambda nzc=nzc, wid=wid: v.tensor_scalar(
                                zhi[0:nzc, 0:wid], zt[0:nzc, 0:wid], 4, LOMASK,
                                ALU.logical_shift_right, ALU.bitwise_and))
                            for w8 in range(8):
                                half, byt = (zlo, w8 // 2) if w8 % 2 == 0 else (zhi, w8 // 2)
                                b.op(lambda dst=dst, nzc=nzc, wid=wid, w8=w8, half=half, byt=byt:
                                     v.scalar_tensor_tensor(
                                         [szc_g, szc_u, szc_d][dst][0:nzc, w8::8],
                                         half[0:nzc, 0:wid].bitcast(U8)[:, byt::4],
                                         1.0, sraw[0:nzc, w8:wid * 8:8], ALU.add, ALU.mult))
                            b.mark(name, "v")

                    def t2(t_):
                        b.barrier_wait(t_, "p1")
                        for t in range(NTT):
                            b.wait(t_, f"xgb{t}")
                            if t >= 1:
                                b.wait(t_, f"xgc{t-1}")
                            b.op(lambda: t_.transpose(ps_xg[:], ggb[:, 0:32], pid[:]))
                            b.mark(f"xgt{t}", "pe")

                    def s2q(s):
                        b.barrier_wait(s, "p1")
                        for t in range(NTT):
                            b.wait(s, f"x1a{t}")
                            b.dma(lambda t=t: s.dma_start(
                                out=X1BF[t * 128:(t + 1) * 128, :],
                                in_=x1p[:, (t % 2) * D:((t % 2) + 1) * D]), f"swr{t % 2}")
                            b.mark(f"x1s{t}", f"swr{t % 2}")

                    _sections(b, blk, [("gpsimd", g2), ("scalar", ca2),
                                       ("vector", v2), ("tensor", t2), ("sync", s2q)])
                    b.barrier_mark("p2")

                if stop == "p2":
                    def finp2(g):
                        b.barrier_wait(g, "p2")
                    _sections(b, blk, [("gpsimd", finp2)])
                    continue

                with ExitStack() as s3:
                    if not dry:
                        e3 = s3.enter_context
                        x1T = e3(nc.sbuf_tensor("x1T", [128, KT * 1024], BF16))
                        qstg = e3(nc.sbuf_tensor("qstg", [128, 2 * 512], I32))
                        qlo = e3(nc.sbuf_tensor("qlo", [128, 512], I32))
                        qhi = e3(nc.sbuf_tensor("qhi", [128, 512], I32))
                        sestg = e3(nc.sbuf_tensor("sestg", [128, 2 * 512], F32))
                        wring = e3(nc.sbuf_tensor("wring", [128, 3 * 512], BF16))
                        g_sb = e3(nc.sbuf_tensor("g_sb", [128, NTT * 512], BF16))
                        hpair = e3(nc.sbuf_tensor("hpair", [128, 2 * 8192], BF16))
                        hg1 = e3(nc.sbuf_tensor("hg1", [128, 64], F32))
                        ps_hg = None
                    else:
                        x1T = qstg = qlo = qhi = sestg = wring = g_sb = hpair = hg1 = ps_hg = None

                    QW = {0: GQ, 1: UQ}
                    SE = {0: GS, 1: US}
                    SZ = {0: szc_g, 1: szc_u}

                    def s3q(s):
                        b.barrier_wait(s, "p2")
                        for tau in range(KT):
                            b.dma(lambda tau=tau: s.dma_start(
                                out=x1T[:, tau * 1024:(tau + 1) * 1024],
                                in_=X1BF[:, tau * 128:(tau + 1) * 128], transpose=True), "sxt")
                            b.mark(f"x1T{tau}", "sxt")
                        for m2 in range(NM2):
                            par = m2 % 2
                            for t in range(NTT):
                                b.wait(s, f"uev{2*m2}_{t}")
                                if 2 * m2 + 1 < NIC:
                                    b.wait(s, f"uev{2*m2+1}_{t}")
                                b.dma(lambda m2=m2, t=t, par=par: s.dma_start(
                                    out=HSP[m2 * 1024 + t * 128:m2 * 1024 + (t + 1) * 128, :],
                                    in_=hpair[:, par * 8192 + t * 1024:par * 8192 + (t + 1) * 1024]), f"sxb{par}")
                                b.mark(f"hspl{m2}_{t}", f"sxb{par}")

                    def g3(g):
                        b.barrier_wait(g, "p2")
                        gl = 0
                        for ic in range(NIC):
                            off, wid = ICS[ic]
                            for ku in range(2):
                                for m in range(4):
                                    sl = gl % 2
                                    if gl >= 2:
                                        b.wait(g, f"ex{gl-2}")
                                    b.dma(lambda sl=sl, ku=ku, m=m, off=off, wid=wid: g.dma_start(
                                        out=qstg[:, sl * 512:sl * 512 + wid],
                                        in_=QW[ku][m * 128:(m + 1) * 128, off:off + wid]), f"gx{sl}")
                                    b.mark(f"q{gl}", f"gx{sl}")
                                    if gl >= 2:
                                        b.wait(g, f"wt{gl-2}_7")
                                    b.dma(lambda sl=sl, ku=ku, m=m, off=off, wid=wid: g.dma_start(
                                        out=sestg[:, sl * 512:sl * 512 + wid],
                                        in_=SE[ku][m * 128:(m + 1) * 128, off:off + wid]), f"gw{sl}")
                                    b.mark(f"se{gl}", f"gw{sl}")
                                    gl += 1

                    def v3(v):
                        b.barrier_wait(v, "p2")
                        gl = 0
                        gw = 0
                        for ic in range(NIC):
                            off, wid = ICS[ic]
                            m2 = ic // 2
                            par = m2 % 2
                            for ku in range(2):
                                for m in range(4):
                                    sl = gl % 2
                                    b.wait(v, f"q{gl}")
                                    b.op(lambda sl=sl, wid=wid: v.tensor_scalar(
                                        qlo[:, 0:wid], qstg[:, sl * 512:sl * 512 + wid],
                                        LOMASK, None, ALU.bitwise_and))
                                    b.op(lambda sl=sl, wid=wid: v.tensor_scalar(
                                        qhi[:, 0:wid], qstg[:, sl * 512:sl * 512 + wid],
                                        4, LOMASK, ALU.logical_shift_right, ALU.bitwise_and))
                                    b.mark(f"ex{gl}", "v")
                                    b.wait(v, f"se{gl}")
                                    for j in range(8):
                                        half, byt = (qlo, j // 2) if j % 2 == 0 else (qhi, j // 2)
                                        ws = gw % 3
                                        if gw >= 3:
                                            b.wait(v, f"mmw{gw-3}")
                                        b.op(lambda half=half, byt=byt, ws=ws, sl=sl, wid=wid:
                                             v.tensor_tensor(
                                                 wring[:, ws * 512:ws * 512 + wid],
                                                 half[:, 0:wid].bitcast(U8)[:, byt::4],
                                                 sestg[:, sl * 512:sl * 512 + wid], ALU.mult))
                                        b.mark(f"wt{gl}_{j}", "v")
                                        gw += 1
                                    gl += 1
                                if ku == 1:
                                    if ic == 20:
                                        for t in range(NTT):
                                            if m2 >= 2:
                                                b.wait(v, f"hspl{m2-2}_{NTT-1}")
                                                b.wait(v, f"hgr{m2-2}_{t}")
                                            b.op(lambda t=t, par=par: v.memset(
                                                hpair[:, par * 8192 + t * 1024:par * 8192 + (t + 1) * 1024], 0.0))
                                        b.mark("hp10z", "v")
                                    for t in range(NTT):
                                        b.wait(v, f"ucor{ic}_{t}")
                                        if ic % 2 == 0 and m2 >= 2 and ic != 20:
                                            b.wait(v, f"hspl{m2-2}_{NTT-1}")
                                            b.wait(v, f"hgr{m2-2}_{t}")
                                        U = wid // 8
                                        cof = 64 * (ic % 2)
                                        b.op(lambda t=t, par=par, U=U, cof=cof, wid=wid: v.tensor_tensor(
                                            hpair[:, par * 8192 + t * 1024:par * 8192 + (t + 1) * 1024]
                                            .rearrange("p (vv uu) -> p uu vv", uu=128)[:, cof:cof + U, :],
                                            psg[t][:, 0:wid].rearrange("p (uu vv) -> p uu vv", vv=8),
                                            g_sb[:, t * 512:t * 512 + wid].rearrange("p (uu vv) -> p uu vv", vv=8),
                                            ALU.mult))
                                        b.mark(f"uev{ic}_{t}", "v")
                            if ic % 2 == 1 or ic == NIC - 1:
                                for t in range(NTT):
                                    b.op(lambda t=t, par=par: v.tensor_reduce(
                                        hg1[:].rearrange("p (cc ww vv) -> p cc ww vv", cc=2, ww=4),
                                        hpair[:, par * 8192 + t * 1024:par * 8192 + (t + 1) * 1024]
                                        .rearrange("p (vv cc ww ux) -> p cc ww vv ux", vv=8, cc=2, ww=4),
                                        AX.X, ALU.add))
                                    b.op(lambda t=t, m2=m2: v.tensor_reduce(
                                        hg_tok[:, t * 88 + m2 * 8:t * 88 + m2 * 8 + 8],
                                        hg1[:].rearrange("p (cw vv) -> p cw vv", vv=8), AX.X, ALU.add))
                                    b.mark(f"hgr{m2}_{t}", "v")

                    def a3(a):
                        b.barrier_wait(a, "p2")
                        for ic in range(NIC):
                            off, wid = ICS[ic]
                            for t in range(NTT):
                                b.wait(a, f"gcor{ic}_{t}")
                                b.op(lambda t=t, wid=wid: a.activation(
                                    g_sb[:, t * 512:t * 512 + wid], psg[t][:, 0:wid], AF.Silu))
                                b.mark(f"sil{ic}_{t}", "a")

                    def t3(t_):
                        b.barrier_wait(t_, "p2")
                        gl = 0
                        gw = 0
                        for ic in range(NIC):
                            off, wid = ICS[ic]
                            for ku in range(2):
                                for m in range(4):
                                    for j in range(8):
                                        tau = 4 * j + m
                                        ws = gw % 3
                                        b.wait(t_, f"wt{gl}_{j}")
                                        for t in range(NTT):
                                            if m == 0 and j == 0:
                                                if ku == 0:
                                                    if ic >= 1:
                                                        b.wait(t_, f"uev{ic-1}_{t}")
                                                else:
                                                    b.wait(t_, f"sil{ic}_{t}")
                                            if ic == 0 and ku == 0 and m == 0 and j == 0 and t == 0:
                                                b.wait(t_, f"x1T{KT-1}")
                                            b.op(lambda tau=tau, t=t, ws=ws, wid=wid: t_.matmul(
                                                psg[t][:, 0:wid],
                                                x1T[:, tau * 1024 + t * 128:tau * 1024 + (t + 1) * 128],
                                                wring[:, ws * 512:ws * 512 + wid],
                                                start=(m == 0 and j == 0), stop=False),
                                                sem="pe" if t == NTT - 1 else None)
                                        b.mark(f"mmw{gw}", "pe")
                                        gw += 1
                                    gl += 1
                                for t in range(NTT):
                                    if ku == 0 and ic == 0 and t == 0:
                                        b.wait(t_, "xgc7")
                                    b.op(lambda t=t, ku=ku, off=off, wid=wid: t_.matmul(
                                        psg[t][:, 0:wid], xgsT[:, t * 128:(t + 1) * 128],
                                        SZ[ku][:, off:off + wid], start=False, stop=True), sem="pe")
                                    b.mark(f"{'gcor' if ku == 0 else 'ucor'}{ic}_{t}", "pe")

                    with ExitStack() as s3p:
                        if not dry:
                            psg = [s3p.enter_context(nc.psum_tensor(f"psg{i}", [128, 512], F32))
                                   for i in range(NTT)]
                        else:
                            psg = [None] * NTT
                        _sections(b, blk, [("sync", s3q), ("gpsimd", g3), ("vector", v3),
                                           ("scalar", a3), ("tensor", t3)])
                    b.barrier_mark("p3")

            if stop == "p3":
                def finp3(g):
                    b.barrier_wait(g, "p3")
                _sections(b, blk, [("gpsimd", finp3)])
                continue

            with ExitStack() as sc0:
                if not dry:
                    R2 = sc0.enter_context(nc.sbuf_tensor("R2", [128, NTT * D], F32))
                else:
                    R2 = None

                with ExitStack() as sc1:
                    if not dry:
                        ec = sc1.enter_context
                        hT = ec(nc.sbuf_tensor("hT", [128, 2 * 8192], BF16))
                        qdst = ec(nc.sbuf_tensor("qdst", [128, 2 * 512], I32))
                        sedst = ec(nc.sbuf_tensor("sedst", [128, 2 * 512], F32))
                        qdlo = ec(nc.sbuf_tensor("qdlo", [128, 512], I32))
                        qdhi = ec(nc.sbuf_tensor("qdhi", [128, 512], I32))
                        wdring = ec(nc.sbuf_tensor("wdring", [128, 3 * 512], BF16))
                        psd = [ec(nc.psum_tensor(f"psd{i}", [128, 512], F32)) for i in range(NTT)]
                    else:
                        hT = qdst = sedst = qdlo = qdhi = wdring = None
                        psd = [None] * NTT

                    def s4(s):
                        b.barrier_wait(s, "p3")
                        for m2 in range(NM2):
                            par = m2 % 2
                            for j2 in range(8):
                                if j2 == 0:
                                    b.wait(s, f"hspl{m2}_{NTT-1}")
                                    if m2 >= 2:
                                        b.wait(s, f"dmml{m2-2}")
                                b.dma(lambda m2=m2, j2=j2, par=par: s.dma_start(
                                    out=hT[:, par * 8192 + j2 * 1024:par * 8192 + (j2 + 1) * 1024],
                                    in_=HSP[m2 * 1024:(m2 + 1) * 1024, j2 * 128:(j2 + 1) * 128],
                                    transpose=True), f"swb{par}")
                                b.mark(f"hT{m2}_{j2}", f"swb{par}")

                    def g4(g):
                        b.barrier_wait(g, "p3")
                        gd = 0
                        for m2 in range(NM2):
                            rows = 96 if m2 == NM2 - 1 else 128
                            for nc_i in range(NCH):
                                sl = gd % 2
                                if gd >= 2:
                                    b.wait(g, f"dex{gd-2}")
                                b.dma(lambda sl=sl, m2=m2, nc_i=nc_i, rows=rows: g.dma_start(
                                    out=qdst[0:rows, sl * 512:(sl + 1) * 512],
                                    in_=DQ[m2 * 128:m2 * 128 + rows, nc_i * 512:(nc_i + 1) * 512]), f"gx{sl}")
                                b.mark(f"qd{gd}", f"gx{sl}")
                                if gd >= 2:
                                    b.wait(g, f"wd{gd-2}_7")
                                b.dma(lambda sl=sl, m2=m2, nc_i=nc_i, rows=rows: g.dma_start(
                                    out=sedst[0:rows, sl * 512:(sl + 1) * 512],
                                    in_=DS[m2 * 128:m2 * 128 + rows, nc_i * 512:(nc_i + 1) * 512]), f"gw{sl}")
                                b.mark(f"sed{gd}", f"gw{sl}")
                                gd += 1

                    def v4(v):
                        b.barrier_wait(v, "p3")
                        for t in range(NTT):
                            if t >= 1:
                                b.wait(v, f"hgt{t-1}")
                            b.op(lambda t=t: v.tensor_copy(ggb[:, 0:86], hg_tok[:, t * 88:t * 88 + 86]))
                            b.mark(f"hgb{t}", "v")
                            b.wait(v, f"hgt{t}")
                            b.op(lambda t=t: v.tensor_scalar_mul(
                                hgsT[:, t * 128:(t + 1) * 128],
                                psd[0][:].bitcast(BF16)[0:86, 0:128], -1.0))
                            b.mark(f"hgc{t}", "v")
                        gd = 0
                        gw = 0
                        for m2 in range(NM2):
                            rows = 96 if m2 == NM2 - 1 else 128
                            for nc_i in range(NCH):
                                sl = gd % 2
                                b.wait(v, f"qd{gd}")
                                b.op(lambda sl=sl, rows=rows: v.tensor_scalar(
                                    qdlo[0:rows, :], qdst[0:rows, sl * 512:(sl + 1) * 512],
                                    LOMASK, None, ALU.bitwise_and))
                                b.op(lambda sl=sl, rows=rows: v.tensor_scalar(
                                    qdhi[0:rows, :], qdst[0:rows, sl * 512:(sl + 1) * 512],
                                    4, LOMASK, ALU.logical_shift_right, ALU.bitwise_and))
                                b.mark(f"dex{gd}", "v")
                                b.wait(v, f"sed{gd}")
                                for j2 in range(8):
                                    half, byt = (qdlo, j2 // 2) if j2 % 2 == 0 else (qdhi, j2 // 2)
                                    ws = gw % 3
                                    if gw >= 3:
                                        b.wait(v, f"dmw{gw-3}")
                                    b.op(lambda half=half, byt=byt, ws=ws, sl=sl, rows=rows:
                                         v.tensor_tensor(
                                             wdring[0:rows, ws * 512:(ws + 1) * 512],
                                             half[0:rows, :].bitcast(U8)[:, byt::4],
                                             sedst[0:rows, sl * 512:(sl + 1) * 512], ALU.mult))
                                    b.mark(f"wd{gd}_{j2}", "v")
                                    gw += 1
                                for t in range(NTT):
                                    b.wait(v, f"dstop{m2}_{nc_i}_{t}")
                                    if m2 == 0:
                                        b.op(lambda t=t, nc_i=nc_i: v.tensor_copy(
                                            R2[:, t * D + nc_i * 512:t * D + (nc_i + 1) * 512],
                                            psd[t][:]))
                                    else:
                                        b.op(lambda t=t, nc_i=nc_i: v.tensor_tensor(
                                            R2[:, t * D + nc_i * 512:t * D + (nc_i + 1) * 512],
                                            psd[t][:],
                                            R2[:, t * D + nc_i * 512:t * D + (nc_i + 1) * 512],
                                            ALU.add))
                                    b.mark(f"dev{m2}_{nc_i}_{t}", "v")
                                gd += 1

                    def t4(t_):
                        b.barrier_wait(t_, "p3")
                        for t in range(NTT):
                            b.wait(t_, f"hgb{t}")
                            if t >= 1:
                                b.wait(t_, f"hgc{t-1}")
                            b.op(lambda: t_.transpose(
                                psd[0][:].bitcast(BF16)[0:86, 0:128], ggb[:, 0:86], pid[:]))
                            b.mark(f"hgt{t}", "pe")
                        gd = 0
                        gw = 0
                        for m2 in range(NM2):
                            rows = 96 if m2 == NM2 - 1 else 128
                            par = m2 % 2
                            for nc_i in range(NCH):
                                for j2 in range(8):
                                    ws = gw % 3
                                    b.wait(t_, f"wd{gd}_{j2}")
                                    if nc_i == 0 and j2 == 0:
                                        b.wait(t_, f"hT{m2}_{7}")
                                    for t in range(NTT):
                                        if j2 == 0:
                                            if m2 == 0 and nc_i == 0:
                                                if t == 0:
                                                    b.wait(t_, f"hgc{NTT-1}")
                                            elif nc_i == 0:
                                                b.wait(t_, f"dev{m2-1}_{NCH-1}_{t}")
                                            else:
                                                b.wait(t_, f"dev{m2}_{nc_i-1}_{t}")
                                        last = (j2 == 7 and m2 != NM2 - 1)
                                        b.op(lambda t=t, j2=j2, ws=ws, par=par, rows=rows, last=last:
                                             t_.matmul(
                                                 psd[t][:],
                                                 hT[0:rows, par * 8192 + j2 * 1024 + t * 128:
                                                    par * 8192 + j2 * 1024 + (t + 1) * 128],
                                                 wdring[0:rows, ws * 512:(ws + 1) * 512],
                                                 start=(j2 == 0), stop=last),
                                             sem="pe" if (last or t == NTT - 1) else None)
                                        if last:
                                            b.mark(f"dstop{m2}_{nc_i}_{t}", "pe")
                                    b.mark(f"dmw{gw}", "pe")
                                    gw += 1
                                if m2 == NM2 - 1:
                                    for t in range(NTT):
                                        if nc_i == 0 and t == 0:
                                            b.wait(t_, f"hgc{NTT-1}")
                                        b.op(lambda t=t, nc_i=nc_i: t_.matmul(
                                            psd[t][:], hgsT[:, t * 128:(t + 1) * 128],
                                            szc_d[:, nc_i * 512:(nc_i + 1) * 512],
                                            start=False, stop=True), sem="pe")
                                        b.mark(f"dstop{m2}_{nc_i}_{t}", "pe")
                                gd += 1
                            b.mark(f"dmml{m2}", "pe")

                    _sections(b, blk, [("sync", s4), ("gpsimd", g4),
                                       ("vector", v4), ("tensor", t4)])
                    b.barrier_mark("p4")

                if stop == "p4":
                    def finp4(g):
                        b.barrier_wait(g, "p4")
                    _sections(b, blk, [("gpsimd", finp4)])
                    continue

                with ExitStack() as s5:
                    if not dry:
                        ostg = s5.enter_context(nc.sbuf_tensor("ostg", [128, 2 * D], F32))
                        sqs2 = s5.enter_context(nc.sbuf_tensor("sqs2", [128, D], F32))
                        x1f = s5.enter_context(nc.sbuf_tensor("x1f", [128, D], BF16))
                    else:
                        ostg = sqs2 = x1f = None

                    def g5(g):
                        b.barrier_wait(g, "p4")
                        for t in range(NTT):
                            if t >= 1:
                                b.wait(g, f"x1add{t-1}")
                            b.dma(lambda t=t: g.dma_start(
                                out=x1f[:], in_=X1BF[t * 128:(t + 1) * 128, :]), "gx0")
                            b.mark(f"x1l{t}", "gx0")

                    def ca5(c):
                        b.barrier_wait(c, "p4")
                        for t in range(NTT):
                            sl = t % 2
                            b.wait(c, f"x1add{t}")
                            b.op(lambda t=t: c.activation(
                                sqs2[:], R2[:, t * D:(t + 1) * D], AF.Square, accum_out=SC("sq", t)))
                            b.mark(f"sq2_{t}", "a")
                            b.wait(c, f"st1b_{t}")
                            b.op(lambda t=t: c.activation(SC("i0", t), SC("rec", t), AF.Sqrt))
                            b.mark(f"sqrt2_{t}", "a")
                            b.wait(c, f"oact{t}")
                            b.dma(lambda sl=sl, t=t: c.dma_start(
                                out=Y[t * 128:(t + 1) * 128, :], in_=ostg[:, sl * D:(sl + 1) * D]), f"crs{sl}")
                            b.mark(f"yst{t}", f"crs{sl}")
                        b.wait(c, f"yst{NTT-2}")
                        b.wait(c, f"yst{NTT-1}")

                    def v5(v):
                        b.barrier_wait(v, "p4")
                        for t in range(NTT):
                            b.wait(v, f"x1l{t}")
                            b.op(lambda t=t: v.tensor_tensor(
                                R2[:, t * D:(t + 1) * D].rearrange("p (u v) -> p v u", v=8),
                                R2[:, t * D:(t + 1) * D].rearrange("p (u v) -> p v u", v=8),
                                x1f[:].rearrange("p (v u) -> p v u", u=512),
                                ALU.add))
                            b.mark(f"x1add{t}", "v")
                            b.wait(v, f"sq2_{t}")
                            b.op(lambda t=t: v.reduce_sum(SC("sum", t), R2[:, t * D:(t + 1) * D], AX.X))
                            b.op(lambda t=t: v.tensor_scalar_mul(SC("mean", t), SC("sum", t), 1.0 / D))
                            b.op(lambda t=t: v.tensor_mul(SC("msq", t), SC("mean", t), SC("mean", t)))
                            b.op(lambda t=t: v.scalar_tensor_tensor(
                                SC("ve", t), SC("sq", t), 1.0 / D, SC("msq", t), ALU.mult, ALU.subtract))
                            b.op(lambda t=t: v.tensor_scalar_add(SC("ve", t), SC("ve", t), EPS))
                            b.op(lambda t=t: v.reciprocal(SC("rec", t), SC("ve", t)))
                            b.mark(f"st1b_{t}", "v")
                            b.wait(v, f"sqrt2_{t}")
                            b.op(lambda t=t: v.tensor_mul(SC("ta", t), SC("i0", t), SC("i0", t)))
                            b.op(lambda t=t: v.tensor_mul(SC("ta", t), SC("ta", t), SC("ve", t)))
                            b.op(lambda t=t: v.tensor_scalar(SC("ta", t), SC("ta", t), -0.5, 1.5, ALU.mult, ALU.add))
                            b.op(lambda t=t: v.tensor_mul(SC("inv", t), SC("i0", t), SC("ta", t)))
                            b.op(lambda t=t: v.tensor_scalar_mul(SC("nsb", t), SC("mean", t), -1.0))
                            b.mark(f"st2b_{t}", "v")
                            if t >= 2:
                                b.wait(v, f"yst{t-2}")
                            b.op(lambda t=t: v.tensor_scalar(
                                ostg[:, (t % 2) * D:((t % 2) + 1) * D], R2[:, t * D:(t + 1) * D],
                                SC("nsb", t), SC("inv", t), ALU.add, ALU.mult))
                            b.mark(f"oact{t}", "v")

                    _sections(b, blk, [("gpsimd", g5), ("scalar", ca5), ("vector", v5)])
    return nc


def kernel(L0_attn_data, L0_attn_weight, L0_attn_mask,
           L0_ffn_gate_qweight, L0_ffn_gate_scales, L0_ffn_gate_qzeros, L0_ffn_gate_g_idx,
           L0_ffn_up_qweight, L0_ffn_up_scales, L0_ffn_up_qzeros, L0_ffn_up_g_idx,
           L0_ffn_down_qweight, L0_ffn_down_scales, L0_ffn_down_qzeros, L0_ffn_down_g_idx):
    bf = ml_dtypes.bfloat16
    x = np.ascontiguousarray(np.asarray(L0_attn_data, np.float32).reshape(8192, D))
    gsc = np.asarray(L0_ffn_gate_scales, np.float32)
    usc = np.asarray(L0_ffn_up_scales, np.float32)
    dsc = np.asarray(L0_ffn_down_scales, np.float32)

    pid = np.eye(128, dtype=bf)
    sel = np.zeros((128, KT * 128), np.float32)
    kt_idx = np.arange(KT)[:, None]
    p_idx = np.arange(128)[None, :]
    rows = (4 * kt_idx + p_idx // 32)
    cols = 128 * kt_idx + p_idx
    sel[rows.ravel(), cols.ravel()] = 1.0
    sel = sel.astype(bf)

    common = dict(
        w=np.ascontiguousarray(np.asarray(L0_attn_weight, np.float32)),
        mk=np.ascontiguousarray(np.asarray(L0_attn_mask, np.int32)),
        gq=np.ascontiguousarray(np.asarray(L0_ffn_gate_qweight, np.int32)),
        uq=np.ascontiguousarray(np.asarray(L0_ffn_up_qweight, np.int32)),
        dq=np.ascontiguousarray(np.asarray(L0_ffn_down_qweight, np.int32)),
        gs=np.ascontiguousarray(np.repeat(gsc, 16, axis=0)),
        us=np.ascontiguousarray(np.repeat(usc, 16, axis=0)),
        ds=np.ascontiguousarray(np.repeat(dsc, 16, axis=0)),
        gz=np.ascontiguousarray(np.asarray(L0_ffn_gate_qzeros, np.int32)),
        uz=np.ascontiguousarray(np.asarray(L0_ffn_up_qzeros, np.int32)),
        dz=np.ascontiguousarray(np.asarray(L0_ffn_down_qzeros, np.int32)),
        gsr=np.ascontiguousarray(gsc),
        usr=np.ascontiguousarray(usc),
        dsr=np.ascontiguousarray(dsc),
        pid=pid, sel=sel,
    )
    in_maps = [dict(common, x=np.ascontiguousarray(x[T * c:T * (c + 1)])) for c in range(8)]
    nc = _build()
    res = run_bass_kernel_spmd(nc, in_maps, core_ids=list(range(8)))
    y = np.concatenate([np.asarray(r["y"], np.float32) for r in res.results], axis=0)
    return y.reshape(1, 8192, D)
